# version 08
# baseline (speedup 1.0000x reference)
import math
import os

import numpy as np
import ml_dtypes
import concourse.bass as bass
import concourse.mybir as mybir
from concourse.bass_utils import run_bass_kernel_spmd

F32 = mybir.dt.float32
F16 = mybir.dt.float16
BF16 = mybir.dt.bfloat16
I32 = mybir.dt.int32
I16 = mybir.dt.int16
AF = mybir.ActivationFunctionType
OP = mybir.AluOpType

D = 2048
DC = 16
MLP = 16384
NB = 128
GRP = 4
NG = NB // GRP
HEADS = 16
TOT = 1280
OWN = 1024
T3 = [(0, 512), (512, 512), (1024, 256)]
T2 = [(256, 512), (768, 512)]
RS = 1.0 / math.sqrt(128.0)

ENGINES = ("sync", "vector", "scalar", "tensor", "gpsimd")
_DEBUG = int(os.environ.get("K_DEBUG", "0"))


class Ctr:
    __slots__ = ("h", "n", "qeng", "maxw", "gated")

    def __init__(self, h, qeng=None):
        self.h = h
        self.n = 0
        self.qeng = qeng
        self.maxw = 0
        self.gated = 0

    def inc(self, k=1):
        self.n += k
        return self.n


class Prog:
    def __init__(self):
        self.ops = {e: [] for e in ENGINES}
        self.wmax = {}

    def em(self, eng, fn):
        self.ops[eng].append(fn)

    def wait(self, eng, ctr, val):
        if val <= 0:
            return
        if val > ctr.maxw:
            ctr.maxw = val
        if eng == ctr.qeng and val > ctr.gated:
            ctr.gated = val
        key = (eng, id(ctr))
        if self.wmax.get(key, -1) >= val:
            return
        self.wmax[key] = val
        h = ctr.h
        self.em(eng, lambda e, h=h, v=val: e.wait_ge(h, v))


class Res:
    __slots__ = ("deps",)

    def __init__(self):
        self.deps = []

    def use(self, P, eng):
        for (c, v) in self.deps:
            P.wait(eng, c, v)
        self.deps = []

    def freed_by(self, ctr, v):
        if ctr.qeng is not None and v > ctr.maxw:
            ctr.maxw = v
        self.deps.append((ctr, v))


def build_nc():
    nc = bass.Bass("TRN2", target_bir_lowering=False)

    xin = nc.dram_tensor("xin", [TOT, D], F32, kind="ExternalInput").ap()
    wq_d = nc.dram_tensor("wq", [D, D], F32, kind="ExternalInput").ap()
    wk_d = nc.dram_tensor("wk", [D, D], F32, kind="ExternalInput").ap()
    wv_d = nc.dram_tensor("wv", [D, D], F32, kind="ExternalInput").ap()
    wo_d = nc.dram_tensor("wo", [D, D], F32, kind="ExternalInput").ap()
    gp_d = nc.dram_tensor("gp", [NB, 128, 1024], I32, kind="ExternalInput").ap()
    gs_d = nc.dram_tensor("gs", [NB, 8, 8, 128], F32, kind="ExternalInput").ap()
    up_d = nc.dram_tensor("up", [NB, 128, 1024], I32, kind="ExternalInput").ap()
    us_d = nc.dram_tensor("us", [NB, 8, 8, 128], F32, kind="ExternalInput").ap()
    dp_d = nc.dram_tensor("dp", [64, 128, 2048], I32, kind="ExternalInput").ap()
    ds_d = nc.dram_tensor("ds", [64, 8, 16, 128], F32, kind="ExternalInput").ap()
    ch_d = nc.dram_tensor("cosh", [128, TOT], F32, kind="ExternalInput").ap()
    cl_d = nc.dram_tensor("cosl", [128, TOT], F32, kind="ExternalInput").ap()
    sh_d = nc.dram_tensor("sinh", [128, TOT], F32, kind="ExternalInput").ap()
    sl_d = nc.dram_tensor("sinl", [128, TOT], F32, kind="ExternalInput").ap()
    msk_d = nc.dram_tensor("msk", [128, 4, 2, 512], BF16, kind="ExternalInput").ap()

    y_d = nc.dram_tensor("y", [OWN, D], F32, kind="ExternalOutput").ap()
    taps = {}
    if _DEBUG:
        for nm, dt in (("t_ln1", BF16), ("t_x1", F32), ("t_ln2", BF16), ("t_x2", F32), ("t_ln3", BF16)):
            taps[nm] = nc.dram_tensor(nm, [128, DC * TOT], dt, kind="ExternalOutput").ap()

    scrG = nc.dram_tensor("scrG", [NB, 128, 2048], BF16, kind="Internal").ap()
    scrU = nc.dram_tensor("scrU", [NB, 128, 2048], BF16, kind="Internal").ap()
    scrD = nc.dram_tensor("scrD", [NB, 128, 2048], BF16, kind="Internal").ap()
    scrAq = nc.dram_tensor("scrAq", [16, 128, 16, 128], BF16, kind="Internal").ap()
    scrAk = nc.dram_tensor("scrAk", [16, 128, 16, 128], BF16, kind="Internal").ap()
    scrAv = nc.dram_tensor("scrAv", [16, 128, D], BF16, kind="Internal").ap()
    scrAo = nc.dram_tensor("scrAo", [16, 128, 16, 128], BF16, kind="Internal").ap()

    iden_d = nc.inline_tensor(np.eye(128, dtype=np.float32), name="idn").ap()
    onesc_d = nc.inline_tensor(np.ones((128, 1), np.float32), name="onc").ap()
    onesr_d = nc.inline_tensor(np.ones((1, 128), np.float32), name="onr").ap()
    cvals = np.zeros((128, 8), np.float32)
    cvals[:, 0] = -4.0
    cvals[:, 1] = -12.0
    cvals[:, 2] = 0.5
    cvals[:, 3] = -0.125
    cvals[:, 4] = 1e-5
    cvals[:, 5] = 2.0
    cblk_d = nc.inline_tensor(cvals, name="cbl").ap()

    TOPA = 229344
    FLOOR = 16500
    cur = [TOPA]
    szof = {F32: 4, F16: 2, BF16: 2, I32: 4, I16: 2}

    def alloc(name, shape, dt):
        nbytes = math.prod(shape[1:]) * szof[dt]
        nbytes = (nbytes + 127) // 128 * 128
        cur[0] -= nbytes
        assert cur[0] >= FLOOR, f"SBUF overflow at {name}: {cur[0]}"
        return nc.alloc_sbuf_tensor_at(name, shape, dt, offset=cur[0])

    x = alloc("x", [128, DC, TOT], F32)
    lnb = alloc("lnb", [128, DC, TOT], BF16)
    iden = alloc("iden", [128, 128], F32)
    onesc = alloc("onesc", [128, 1], F32)
    onesr = alloc("onesr", [1, 128], F32)
    cblk = alloc("cblk", [128, 8], F32)

    AR_TOP = cur[0]

    def arena():
        cur[0] = AR_TOP

    arena()
    pk_gu = alloc("pk_gu", [128, 2, 1024], I16)
    sc_gu = alloc("sc_gu", [128, 2, 1024], F32)
    w_gu = alloc("w_gu", [128, 4, 2, 1024], BF16)
    pk_dn = alloc("pk_dn", [128, 2, 1024], I16)
    sc_dn = alloc("sc_dn", [128, 1, 1024], F32)
    w_dn = alloc("w_dn", [128, 3, 2, 1024], BF16)
    c7b = alloc("c7b", [128, 2, 1024], I16)
    s8b = alloc("s8b", [128, 2, 1024], I16)
    r1b = alloc("r1b", [128, 3, 1024], F16)
    r2b = alloc("r2b", [128, 3, 1024], F16)
    sgb = alloc("sgb", [128, 3, 1024], F16)
    inter = alloc("inter", [128, GRP, TOT], BF16)
    gel2 = alloc("gel2", [128, 2, 512], BF16)

    arena()
    kring = alloc("kring", [128, 2, HEADS, 256], BF16)
    vring = alloc("vring", [128, 2, 2, D], BF16)
    qbuf = alloc("qbuf", [128, HEADS, 256], BF16)
    afm = alloc("afm", [128, HEADS, 256], BF16)
    wsl2 = alloc("wsl2", [128, 2, 2048], BF16)
    cs4 = alloc("cs4", [128, 4, 256], F32)
    mskb = alloc("mskb", [128, 2, 512], BF16)
    smb = alloc("smb", [128, 512], F32)
    pnraw = alloc("pnraw", [128, 1, 512], BF16)
    pn2b = alloc("pn2b", [128, 1, 512], BF16)
    pTb = alloc("pTb", [128, 4, 256], BF16)
    tA2 = alloc("tA2", [128, 2, 256], F32)
    tB2 = alloc("tB2", [128, 2, 256], F32)
    rmx = alloc("rmx", [128, 1], F32)
    rsm = alloc("rsm", [128, 1], F32)
    rin = alloc("rin", [128, 1], F32)

    arena()
    xtm2 = alloc("xtm2", [128, 2, 2048], F32)
    abuf2 = alloc("abuf2", [128, 2, 2048], BF16)

    arena()
    stA = alloc("stA", [1, 512], F32)
    stB = alloc("stB", [1, 512], F32)
    stC = alloc("stC", [1, 512], F32)
    stR = alloc("stR", [1, 512], F32)
    sqb = alloc("sqb", [128, 512], F32)
    t1b = alloc("t1b", [128, 512], F32)

    pb = [nc.alloc_psum_tensor("pb%d" % i, [128, 512], F32) for i in range(8)]

    sem_hold = []

    def mksem(name):
        cm = nc.semaphore(name)
        h = cm.__enter__()
        sem_hold.append(cm)
        return Ctr(h)

    cIn = mksem("cIn")
    cPk = mksem("cPk")
    cTr = mksem("cTr")
    cWr = mksem("cWr")
    cCa = mksem("cCa")
    cOut = mksem("cOut")
    cV = mksem("cV")
    cA = mksem("cA")
    cP = mksem("cP")
    cT = mksem("cT")
    cPh = mksem("cPh")
    cTr2 = mksem("cTr2")
    for _c in (cIn, cTr, cTr2, cWr, cOut):
        _c.qeng = "sync"
    cPk.qeng = "scalar"
    cCa.qeng = "gpsimd"

    P = Prog()
    phase = [0]

    banks = [Res() for _ in range(8)]
    r_pkgu = [Res() for _ in range(2)]
    r_scgu = [Res() for _ in range(2)]
    r_wgu = [Res() for _ in range(4)]
    r_pkdn = [Res(), Res()]
    r_scdn = [Res()]
    r_wdn = [Res() for _ in range(3)]
    r_c7 = [Res(), Res()]
    r_s8 = [Res(), Res()]
    r_r1 = [Res() for _ in range(3)]
    r_r2 = [Res() for _ in range(3)]
    r_sg = [Res() for _ in range(3)]
    r_gel = [Res(), Res()]
    r_inter = Res()
    r_sq, r_t1 = Res(), Res()
    r_stA, r_stB, r_stC, r_stR = Res(), Res(), Res(), Res()
    r_xtm = [Res(), Res()]
    r_abuf = [Res(), Res()]
    r_wsl = [Res(), Res()]
    r_cs = Res()
    r_msk = Res()
    r_smb = Res()
    r_pnr = Res()
    r_pn2 = Res()
    r_pT = Res()
    r_tA = [Res(), Res()]
    r_tB = [Res(), Res()]
    r_kring = [Res(), Res()]
    r_vring = [Res(), Res()]
    r_qbuf = Res()
    r_afm = Res()
    r_rmx = Res()
    r_rsm = Res()
    r_rin = Res()
    r_otm = [Res(), Res()]

    def dma(out, in_, ctr, eng="sync"):
        if ctr.maxw > ctr.gated:
            P.wait(ctr.qeng, ctr, ctr.maxw)
        P.em(eng, lambda e, o=out, i=in_, h=ctr.h: e.dma_start(out=o, in_=i).then_inc(h, 16))
        t = ctr.inc(16)
        ctr.maxw = max(ctr.maxw, t)
        return t

    def dma_tr(out, in_, ctr):
        if ctr.maxw > ctr.gated:
            P.wait(ctr.qeng, ctr, ctr.maxw)
        P.em("sync", lambda e, o=out, i=in_, h=ctr.h: e.dma_start(out=o, in_=i, transpose=True).then_inc(h, 16))
        return ctr.inc(16)

    def drain_t():
        P.em("tensor", lambda e, h=cT.h: e.drain().then_inc(h, 1))
        return cT.inc()

    def barrier():
        for c in (cIn, cTr, cTr2, cWr, cOut):
            P.wait("sync", c, c.n)
        P.wait("scalar", cPk, cPk.n)
        P.wait("gpsimd", cCa, cCa.n)
        phase[0] += 1
        tgt = 5 * phase[0]
        for e in ENGINES:
            P.em(e, lambda en, h=cPh.h: en.nop().then_inc(h, 1))
        for e in ENGINES:
            P.wait(e, cPh, tgt)

    def tap(name, src):
        if name in taps:
            t = dma(taps[name], src, cOut)
            P.wait("sync", cOut, t)

    tmp_flip = [0]

    def emit_dequant(pk_ap, sc_ap, w_ap, r_pk, r_sc, r_w, pk_rdy, sc_rdy):
        subs = []
        for nib in (0, 1):
            subs.append((nib, tmp_flip[0] & 1, tmp_flip[0] % 3))
            tmp_flip[0] += 1
        vex = [0] * 2
        aex = [0] * 2
        gex = [0] * 2
        v2x = [0] * 2
        state = {"first_w": True}
        wsub = [0] * 2

        def ext(i):
            nib, p2, p3 = subs[i]
            c7 = c7b[:, p2, :]
            s8 = s8b[:, p2, :]
            r_c7[p2].use(P, "vector")
            r_s8[p2].use(P, "vector")
            P.wait("vector", cCa, pk_rdy)
            if nib == 0:
                P.em("vector", lambda e, o=c7, i=pk_ap: e.tensor_scalar(o, i, 7, None, OP.bitwise_and))
                P.em("vector", lambda e, o=s8, i=pk_ap, h=cV.h: e.tensor_scalar(o, i, 8, None, OP.bitwise_and).then_inc(h, 1))
            else:
                P.em("vector", lambda e, o=c7, i=pk_ap: e.tensor_scalar(o, i, 4, 7, OP.logical_shift_right, OP.bitwise_and))
                P.em("vector", lambda e, o=s8, i=pk_ap, h=cV.h: e.tensor_scalar(o, i, 4, 8, OP.logical_shift_right, OP.bitwise_and).then_inc(h, 1))
            vex[i] = cV.inc()
            if i == 1:
                r_pk.freed_by(cV, vex[i])

        def acts(i):
            nib, p2, p3 = subs[i]
            c7 = c7b[:, p2, :]
            s8 = s8b[:, p2, :]
            r1 = r1b[:, p3, :]
            r2 = r2b[:, p3, :]
            sg = sgb[:, p3, :]
            P.wait("scalar", cV, vex[i])
            r_r1[p3].use(P, "scalar")
            r_r2[p3].use(P, "scalar")
            r_sg[p3].use(P, "scalar")
            P.em("scalar", lambda e, o=r1, i=c7: e.activation(o, i, AF.Relu, bias=cblk[:, 0:1], scale=1.0))
            P.em("scalar", lambda e, o=r2, i=c7: e.activation(o, i, AF.Relu, bias=cblk[:, 1:2], scale=cblk[:, 5:6]))
            P.em("scalar", lambda e, o=sg, i=s8, h=cA.h: e.activation(o, i, AF.Identity, bias=cblk[:, 2:3], scale=cblk[:, 3:4]).then_inc(h, 1))
            aex[i] = cA.inc()
            r_s8[p2].freed_by(cA, aex[i])

        def gadd(i):
            nib, p2, p3 = subs[i]
            c7 = c7b[:, p2, :]
            r1 = r1b[:, p3, :]
            P.wait("gpsimd", cA, aex[i])
            P.em("gpsimd", lambda e, o=r1, b=c7, h=cP.h: e.tensor_tensor(o, o, b, OP.add).then_inc(h, 1))
            gex[i] = cP.inc()
            r_c7[p2].freed_by(cP, gex[i])

        def vmul(i):
            nib, p2, p3 = subs[i]
            r1 = r1b[:, p3, :]
            r2 = r2b[:, p3, :]
            sg = sgb[:, p3, :]
            P.wait("vector", cP, gex[i])
            P.em("vector", lambda e, o=r2, a=r1, b=r2: e.tensor_tensor(o, a, b, OP.add))
            P.em("vector", lambda e, o=sg, a=r2, h=cV.h: e.tensor_tensor(o, a, o, OP.mult).then_inc(h, 1))
            v2x[i] = cV.inc()
            r_r1[p3].freed_by(cV, v2x[i])
            r_r2[p3].freed_by(cV, v2x[i])

        def gfin(i):
            nib, p2, p3 = subs[i]
            sg = sgb[:, p3, :]
            dst = w_ap[:, nib, :]
            P.wait("gpsimd", cV, v2x[i])
            P.wait("gpsimd", cPk, sc_rdy)
            if state["first_w"]:
                r_w.use(P, "gpsimd")
                state["first_w"] = False
            P.em("gpsimd", lambda e, o=dst, a=sg, b=sc_ap, h=cP.h: e.tensor_tensor(o, a, b, OP.mult).then_inc(h, 1))
            wsub[i] = cP.inc()
            r_sg[p3].freed_by(cP, wsub[i])
            if i == 1:
                r_sc.freed_by(cP, wsub[i])

        ext(0)
        acts(0)
        ext(1)
        acts(1)
        gadd(0)
        vmul(0)
        gadd(1)
        gfin(0)
        vmul(1)
        gfin(1)
        return [(cP, wsub[0]), (cP, wsub[0]), (cP, wsub[1]), (cP, wsub[1])]

    def emit_ln(tiles):
        for (t0, T) in tiles:
            for ch in range(DC):
                xs = x[:, ch, t0:t0 + T]
                r_sq.use(P, "scalar")
                P.em("scalar", lambda e, o=sqb[:, 0:T], i=xs, h=cA.h: e.activation(o, i, AF.Square, bias=0.0, scale=1.0).then_inc(h, 1))
                aex = cA.inc()
                if ch == 0:
                    banks[6].use(P, "tensor")
                    banks[7].use(P, "tensor")
                P.em("tensor", lambda e, o=pb[6][0:1, 0:T], i=xs, st=(ch == 0), sp=(ch == DC - 1): e.matmul(o, onesc[:], i, start=st, stop=sp))
                P.wait("tensor", cA, aex)
                P.em("tensor", lambda e, o=pb[7][0:1, 0:T], i=sqb[:, 0:T], st=(ch == 0), sp=(ch == DC - 1), h=cT.h: e.matmul(o, onesc[:], i, start=st, stop=sp).then_inc(h, 1))
                r_sq.freed_by(cT, cT.inc())
            tex = drain_t()
            P.wait("vector", cT, tex)
            r_stA.use(P, "vector")
            r_stC.use(P, "vector")
            r_stB.use(P, "vector")
            P.em("vector", lambda e, o=stA[0:1, 0:T], i=pb[6][0:1, 0:T], h=cV.h: e.tensor_scalar(o, i, -1.0 / 2048.0, None, OP.mult).then_inc(h, 1))
            P.wait("vector", cV, cV.inc())
            P.em("vector", lambda e, o=stC[0:1, 0:T], a=stA[0:1, 0:T], h=cV.h: e.tensor_tensor(o, a, a, OP.mult).then_inc(h, 1))
            P.wait("vector", cV, cV.inc())
            P.em("vector", lambda e, o=stB[0:1, 0:T], i=pb[7][0:1, 0:T], c=stC[0:1, 0:T], h=cV.h: e.scalar_tensor_tensor(o, i, 1.0 / 2048.0, c, OP.mult, OP.subtract).then_inc(h, 1))
            vex = cV.inc()
            banks[6].freed_by(cV, vex)
            banks[7].freed_by(cV, vex)
            P.wait("scalar", cV, vex)
            r_stC.use(P, "scalar")
            P.em("scalar", lambda e, o=stC[0:1, 0:T], i=stB[0:1, 0:T], h=cA.h: e.activation(o, i, AF.Sqrt, bias=cblk[0:1, 4:5], scale=1.0).then_inc(h, 1))
            aex = cA.inc()
            r_stB.freed_by(cA, aex)
            P.wait("vector", cA, aex)
            r_stR.use(P, "vector")
            P.em("vector", lambda e, o=stR[0:1, 0:T], i=stC[0:1, 0:T], h=cV.h: e.reciprocal(o, i).then_inc(h, 1))
            vex = cV.inc()
            r_stC.freed_by(cV, vex)
            P.wait("gpsimd", cV, vex)
            P.em("gpsimd", lambda e, o=stA[0:1, 0:T], b=stR[0:1, 0:T], h=cP.h: e.tensor_tensor(o, o, b, OP.mult).then_inc(h, 1))
            pex = cP.inc()
            banks[4].use(P, "tensor")
            banks[5].use(P, "tensor")
            P.wait("tensor", cV, vex)
            P.em("tensor", lambda e, o=pb[4][:, 0:T], i=stR[0:1, 0:T]: e.matmul(o, onesr[:], i, start=True, stop=True))
            P.wait("tensor", cP, pex)
            P.em("tensor", lambda e, o=pb[5][:, 0:T], i=stA[0:1, 0:T]: e.matmul(o, onesr[:], i, start=True, stop=True))
            tex = drain_t()
            r_stR.freed_by(cT, tex)
            r_stA.freed_by(cT, tex)
            P.wait("vector", cT, tex)
            for ch in range(DC):
                xs = x[:, ch, t0:t0 + T]
                ls = lnb[:, ch, t0:t0 + T]
                r_t1.use(P, "vector")
                P.em("vector", lambda e, o=t1b[:, 0:T], a=xs, b=pb[4][:, 0:T], h=cV.h: e.tensor_tensor(o, a, b, OP.mult).then_inc(h, 1))
                P.wait("vector", cV, cV.inc())
                P.em("vector", lambda e, o=ls, a=t1b[:, 0:T], b=pb[5][:, 0:T], h=cV.h: e.tensor_tensor(o, a, b, OP.add).then_inc(h, 1))
                vex = cV.inc()
                r_t1.freed_by(cV, vex)
            banks[4].freed_by(cV, vex)
            banks[5].freed_by(cV, vex)

    gu_slot = [0]
    dn_slot = [0]
    dpk_slot = [0]
    mmctr = [0]

    def bcast_sc(src3):
        a, b = src3.shape[1], src3.shape[2]
        return src3.rearrange("q a b -> q (a b)").unsqueeze(1).broadcast_to([8, 16, a * b])

    def emit_ffn(deq, tiles):
        nt = len(tiles)
        for g in range(NG):
            r_inter.use(P, "vector")
            inter_rdy = [0] * nt
            for j in range(GRP):
                nb = g * GRP + j
                wrdy = []
                for (pk_hbm, sc_hbm, scr) in ((gp_d, gs_d, scrG), (up_d, us_d, scrU)):
                    slp = gu_slot[0] % 2
                    sl = gu_slot[0] % 4
                    gu_slot[0] += 1
                    if deq:
                        r_pkgu[slp].use(P, "gpsimd")
                        pk_t = dma(pk_gu[:, slp, :], pk_hbm[nb], cCa, eng="gpsimd")
                        r_scgu[slp].use(P, "scalar")
                        sc_t = dma(sc_gu[:, slp, :], bcast_sc(sc_hbm[nb]), cPk, eng="scalar")
                        wsubs = emit_dequant(pk_gu[:, slp, :], sc_gu[:, slp, :], w_gu[:, sl],
                                             r_pkgu[slp], r_scgu[slp], r_wgu[sl], pk_t, sc_t)
                        P.wait("sync", cP, wsubs[3][1])
                        st_t = dma(scr[nb], w_gu[:, sl, :, :], cWr)
                        r_wgu[sl].freed_by(cWr, st_t)
                        wrdy.append((sl, wsubs))
                    else:
                        r_wgu[sl].use(P, "sync")
                        w_t = dma(w_gu[:, sl, :, :], scr[nb], cIn)
                        wrdy.append((sl, [(cIn, w_t)] * 4))
                (slg, g_rdy), (slu, u_rdy) = wrdy
                for ti, (t0, T) in enumerate(tiles):
                    bg = mmctr[0] % 2
                    mmctr[0] += 1
                    bu = 2 + bg
                    for (bk, slw, rdy) in ((bg, slg, g_rdy), (bu, slu, u_rdy)):
                        banks[bk].use(P, "tensor")
                        nmm = 0
                        for si in range(4):
                            P.wait("tensor", rdy[si][0], rdy[si][1])
                            nib, hh = si >> 1, si & 1
                            for c in range(4 * hh, 4 * hh + 4):
                                ck = 2 * c + nib
                                P.em("tensor", lambda e, o=pb[bk][:, 0:T], l=w_gu[:, slw, nib, c * 128:c * 128 + 128], r=lnb[:, ck, t0:t0 + T], st=(nmm == 0), sp=(nmm == DC - 1): e.matmul(o, l, r, start=st, stop=sp))
                                nmm += 1
                    tex = drain_t()
                    if ti == nt - 1:
                        r_wgu[slg].freed_by(cT, tex)
                        r_wgu[slu].freed_by(cT, tex)
                    P.wait("scalar", cT, tex)
                    r_gel[ti % 2].use(P, "scalar")
                    P.em("scalar", lambda e, o=gel2[:, ti % 2, 0:T], i=pb[bg][:, 0:T], h=cA.h: e.activation(o, i, AF.Gelu_apprx_tanh, bias=0.0, scale=1.0).then_inc(h, 1))
                    aex = cA.inc()
                    banks[bg].freed_by(cA, aex)
                    P.wait("vector", cA, aex)
                    P.wait("vector", cT, tex)
                    P.em("vector", lambda e, o=inter[:, j, t0:t0 + T], a=gel2[:, ti % 2, 0:T], b=pb[bu][:, 0:T], h=cV.h: e.tensor_tensor(o, a, b, OP.mult).then_inc(h, 1))
                    vex = cV.inc()
                    banks[bu].freed_by(cV, vex)
                    r_gel[ti % 2].freed_by(cV, vex)
                    inter_rdy[ti] = vex
            last_tex = 0
            for h in (0, 1):
                units = []
                for c2 in (2 * g, 2 * g + 1):
                    dsl = dn_slot[0] % 3
                    dn_slot[0] += 1
                    if deq:
                        psl = dpk_slot[0] % 2
                        dpk_slot[0] += 1
                        r_pkdn[psl].use(P, "gpsimd")
                        pk_t = dma(pk_dn[:, psl, :], dp_d[c2, :, 1024 * h:1024 * (h + 1)], cCa, eng="gpsimd")
                        r_scdn[0].use(P, "scalar")
                        sc_t = dma(sc_dn[:, 0, :], bcast_sc(ds_d[c2, :, 8 * h:8 * (h + 1), :]), cPk, eng="scalar")
                        wsubs = emit_dequant(pk_dn[:, psl, :], sc_dn[:, 0, :], w_dn[:, dsl],
                                             r_pkdn[psl], r_scdn[0], r_wdn[dsl], pk_t, sc_t)
                        P.wait("sync", cP, wsubs[3][1])
                        st_t = dma(scrD[c2 * 2 + h], w_dn[:, dsl, :, :], cWr)
                        r_wdn[dsl].freed_by(cWr, st_t)
                        units.append((dsl, wsubs))
                    else:
                        r_wdn[dsl].use(P, "sync")
                        w_t = dma(w_dn[:, dsl, :, :], scrD[c2 * 2 + h], cIn)
                        units.append((dsl, [(cIn, w_t)] * 4))
                for ti, (t0, T) in enumerate(tiles):
                    for batch in range(4):
                        bis = []
                        for q in (0, 1):
                            ck = 8 * h + 2 * batch + q
                            cki = 2 * batch + q
                            bi = 4 + 2 * (batch & 1) + q
                            bis.append((ck, bi))
                            banks[bi].use(P, "tensor")
                            for jjj in range(4):
                                dsl_u, rdy = units[jjj >> 1]
                                nib = jjj & 1
                                si = nib * 2 + (1 if cki >= 4 else 0)
                                P.wait("tensor", rdy[si][0], rdy[si][1])
                                P.wait("tensor", cV, inter_rdy[ti])
                                P.em("tensor", lambda e, o=pb[bi][:, 0:T], l=w_dn[:, dsl_u, nib, cki * 128:cki * 128 + 128], r=inter[:, jjj, t0:t0 + T], st=(jjj == 0), sp=(jjj == 3): e.matmul(o, l, r, start=st, stop=sp))
                        tex = drain_t()
                        last_tex = tex
                        for (ck, bi) in bis:
                            P.wait("vector", cT, tex)
                            P.em("vector", lambda e, o=x[:, ck, t0:t0 + T], i=pb[bi][:, 0:T], h=cV.h: e.scalar_tensor_tensor(o, i, 0.5, o, OP.mult, OP.add).then_inc(h, 1))
                            vex = cV.inc()
                            banks[bi].freed_by(cV, vex)
                for (dsl_u, rdy) in units:
                    r_wdn[dsl_u].freed_by(cT, last_tex)
            r_inter.freed_by(cT, last_tex)

    rope_flip = [0]

    def emit_rope(ps, dst, tex, cs_t):
        fp = rope_flip[0] & 1
        rope_flip[0] += 1
        tA = tA2[:, fp, :]
        tB = tB2[:, fp, :]
        chs = cs4[:, 0, :]
        cls = cs4[:, 1, :]
        shs = cs4[:, 2, :]
        sls = cs4[:, 3, :]
        P.wait("vector", cT, tex)
        P.wait("vector", cIn, cs_t)
        r_tA[fp].use(P, "vector")
        r_tB[fp].use(P, "vector")
        P.em("vector", lambda e, o=tA, a=ps, b=chs: e.tensor_tensor(o, a, b, OP.mult))
        P.em("vector", lambda e, o=tB, a=ps, b=cls, h=cV.h: e.tensor_tensor(o, a, b, OP.mult).then_inc(h, 1))
        vex = cV.inc()
        P.wait("gpsimd", cV, vex)
        P.em("gpsimd", lambda e, o=tA, b=tB, h=cP.h: e.tensor_tensor(o, o, b, OP.add).then_inc(h, 1))
        pex = cP.inc()
        P.wait("vector", cP, pex)
        P.em("vector", lambda e, o=tB[0:64, :], a=ps[64:128, :], b=shs[0:64, :]: e.tensor_tensor(o, a, b, OP.mult))
        P.em("vector", lambda e, o=tB[64:128, :], a=ps[0:64, :], b=shs[64:128, :], h=cV.h: e.tensor_tensor(o, a, b, OP.mult).then_inc(h, 1))
        vex = cV.inc()
        P.wait("gpsimd", cV, vex)
        P.em("gpsimd", lambda e, o=tA, b=tB, h=cP.h: e.tensor_tensor(o, o, b, OP.add).then_inc(h, 1))
        pex = cP.inc()
        P.wait("vector", cP, pex)
        P.em("vector", lambda e, o=tB[0:64, :], a=ps[64:128, :], b=sls[0:64, :]: e.tensor_tensor(o, a, b, OP.mult))
        P.em("vector", lambda e, o=tB[64:128, :], a=ps[0:64, :], b=sls[64:128, :], h=cV.h: e.tensor_tensor(o, a, b, OP.mult).then_inc(h, 1))
        vex = cV.inc()
        P.wait("gpsimd", cV, vex)
        P.em("gpsimd", lambda e, o=dst, a=tA, b=tB, h=cP.h: e.tensor_tensor(o, a, b, OP.add).then_inc(h, 1))
        pex = cP.inc()
        r_tA[fp].freed_by(cP, pex)
        r_tB[fp].freed_by(cP, pex)
        return vex, pex

    def emit_attn():
        wsl_i = [0]

        def wsl_load(src_region):
            par = wsl_i[0] & 1
            wsl_i[0] += 1
            r_wsl[par].use(P, "sync")
            t = dma(wsl2[:, par, :], src_region, cPk)
            return par, t

        for b in range(5):
            blk0 = 256 * b
            r_cs.use(P, "sync")
            for i, src in enumerate((ch_d, cl_d, sh_d, sl_d)):
                cs_t = dma(cs4[:, i, :], src[:, blk0:blk0 + 256], cIn)
            if b >= 1:
                r_qbuf.use(P, "gpsimd")
                for oc in range(HEADS):
                    par, wt = wsl_load(scrAq[oc])
                    bk = oc % 2
                    banks[bk].use(P, "tensor")
                    P.wait("tensor", cPk, wt)
                    P.wait("tensor", cIn, cs_t)
                    for fc in range(DC):
                        P.em("tensor", lambda e, o=pb[bk][:, 0:256], l=wsl2[:, par, 128 * fc:128 * (fc + 1)], r=lnb[:, fc, blk0:blk0 + 256], st=(fc == 0), sp=(fc == DC - 1): e.matmul(o, l, r, start=st, stop=sp))
                    tex = drain_t()
                    r_wsl[par].freed_by(cT, tex)
                    vex, pex = emit_rope(pb[bk][:, 0:256], qbuf[:, oc, :], tex, cs_t)
                    banks[bk].freed_by(cV, vex)
                q_rdy = cP.n
            sl = b % 2
            r_kring[sl].use(P, "gpsimd")
            for oc in range(HEADS):
                par, wt = wsl_load(scrAk[oc])
                bk = 2 + oc % 2
                banks[bk].use(P, "tensor")
                P.wait("tensor", cPk, wt)
                P.wait("tensor", cIn, cs_t)
                for fc in range(DC):
                    P.em("tensor", lambda e, o=pb[bk][:, 0:256], l=wsl2[:, par, 128 * fc:128 * (fc + 1)], r=lnb[:, fc, blk0:blk0 + 256], st=(fc == 0), sp=(fc == DC - 1): e.matmul(o, l, r, start=st, stop=sp))
                tex = drain_t()
                r_wsl[par].freed_by(cT, tex)
                vex, pex = emit_rope(pb[bk][:, 0:256], kring[:, sl, oc, :], tex, cs_t)
                banks[bk].freed_by(cV, vex)
            k_rdy = cP.n
            r_cs.freed_by(cP, k_rdy)
            r_cs.freed_by(cV, cV.n)
            r_vring[sl].use(P, "scalar")
            for i in range(8):
                banks[i].use(P, "tensor")
            for fc in range(DC):
                par, wt = wsl_load(scrAv[fc])
                P.wait("tensor", cPk, wt)
                for kt2 in range(2):
                    for hq in range(4):
                        bi = kt2 * 4 + hq
                        last = (kt2 == 1 and hq == 3)
                        P.em("tensor", lambda e, o=pb[bi][:, 0:512], l=lnb[:, fc, blk0 + 128 * kt2:blk0 + 128 * (kt2 + 1)], r=wsl2[:, par, 512 * hq:512 * (hq + 1)], st=(fc == 0), sp=(fc == DC - 1), lt=last, h=cT.h: (e.matmul(o, l, r, start=st, stop=sp).then_inc(h, 1) if lt else e.matmul(o, l, r, start=st, stop=sp)))
                r_wsl[par].freed_by(cT, cT.inc())
            tex = drain_t()
            P.wait("scalar", cT, tex)
            for kt2 in range(2):
                for hq in range(4):
                    bi = kt2 * 4 + hq
                    P.em("scalar", lambda e, o=vring[:, sl, kt2, 512 * hq:512 * (hq + 1)], i=pb[bi][:, 0:512], h=cA.h: e.copy(o, i).then_inc(h, 1))
                    aex = cA.inc()
                    banks[bi].freed_by(cA, aex)
            v_rdy = cA.n
            if b >= 1:
                c = b - 1
                sp_ = (b - 1) % 2
                sb_ = b % 2
                r_msk.use(P, "sync")
                mt = dma(mskb[:], msk_d[:, c, :, :], cIn)
                pv_free = 0
                for h in range(HEADS):
                    for qh in range(2):
                        bk = 2 + qh
                        banks[bk].use(P, "tensor")
                        if h == 0 and qh == 0:
                            P.wait("tensor", cP, q_rdy)
                            P.wait("tensor", cP, k_rdy)
                        for si, slot in enumerate((sp_, sb_)):
                            P.em("tensor", lambda e, o=pb[bk][:, 256 * si:256 * (si + 1)], l=qbuf[:, h, 128 * qh:128 * (qh + 1)], r=kring[:, slot, h, :]: e.matmul(o, l, r, start=True, stop=True))
                    tex = drain_t()
                    for qh in range(2):
                        bk = 2 + qh
                        P.wait("vector", cT, tex)
                        P.wait("vector", cIn, mt)
                        r_smb.use(P, "vector")
                        r_rmx.use(P, "vector")
                        P.em("vector", lambda e, o=smb[:], i=pb[bk][:, 0:512], m=mskb[:, qh, :], h=cV.h: e.scalar_tensor_tensor(o, i, RS, m, OP.mult, OP.add).then_inc(h, 1))
                        P.wait("vector", cV, cV.inc())
                        P.em("vector", lambda e, o=rmx[:], i=smb[:], h=cV.h: e.reduce_max(o, i, axis=mybir.AxisListType.X, negate=True).then_inc(h, 1))
                        vex = cV.inc()
                        banks[bk].freed_by(cV, vex)
                        P.wait("scalar", cV, vex)
                        r_pnr.use(P, "scalar")
                        r_rsm.use(P, "scalar")
                        P.em("scalar", lambda e, o=pnraw[:, 0, :], i=smb[:], h=cA.h: e.activation(o, i, AF.Exp, bias=rmx[:], scale=1.0, accum_out=rsm[:]).then_inc(h, 1))
                        aex = cA.inc()
                        r_smb.freed_by(cA, aex)
                        r_rmx.freed_by(cA, aex)
                        P.wait("vector", cA, aex)
                        r_rin.use(P, "vector")
                        P.em("vector", lambda e, o=rin[:], i=rsm[:], h=cV.h: e.reciprocal(o, i).then_inc(h, 1))
                        vex = cV.inc()
                        r_rsm.freed_by(cV, vex)
                        P.wait("scalar", cV, vex)
                        r_pn2.use(P, "scalar")
                        P.em("scalar", lambda e, o=pn2b[:, 0, :], i=pnraw[:, 0, :], h=cA.h: e.activation(o, i, AF.Identity, bias=0.0, scale=rin[:]).then_inc(h, 1))
                        aex = cA.inc()
                        r_pnr.freed_by(cA, aex)
                        r_rin.freed_by(cA, aex)
                        r_pT.use(P, "sync")
                        P.wait("sync", cA, aex)
                        ctr_q = cTr if qh == 0 else cTr2
                        for ktc2 in range(4):
                            dma_tr(pTb[:, ktc2, 128 * qh:128 * (qh + 1)], pn2b[:, 0, 128 * ktc2:128 * (ktc2 + 1)], ctr_q)
                        r_pn2.freed_by(ctr_q, ctr_q.n)
                        if qh == 1:
                            ttr2 = cTr2.n
                    ttr = cTr.n
                    banks[0].use(P, "tensor")
                    P.wait("tensor", cTr, ttr)
                    P.wait("tensor", cTr2, ttr2)
                    P.wait("tensor", cA, v_rdy)
                    for ktc in range(4):
                        slot = (sp_, sp_, sb_, sb_)[ktc]
                        kt2 = ktc % 2
                        P.em("tensor", lambda e, o=pb[0][:, 0:256], l=vring[:, slot, kt2, 128 * h:128 * (h + 1)], r=pTb[:, ktc, :], st=(ktc == 0), sp=(ktc == 3): e.matmul(o, l, r, start=st, stop=sp))
                    tex = drain_t()
                    r_pT.freed_by(cT, tex)
                    if h == 0:
                        r_afm.use(P, "scalar")
                    P.wait("scalar", cT, tex)
                    P.em("scalar", lambda e, o=afm[:, h, :], i=pb[0][:, 0:256], h2=cA.h: e.copy(o, i).then_inc(h2, 1))
                    aex = cA.inc()
                    banks[0].freed_by(cA, aex)
                afm_rdy = cA.n
                r_qbuf.freed_by(cT, cT.n)
                r_msk.freed_by(cV, cV.n)
                for dc in range(DC):
                    par, wt = wsl_load(scrAo[dc])
                    bk = 1 + dc % 2
                    banks[bk].use(P, "tensor")
                    P.wait("tensor", cPk, wt)
                    P.wait("tensor", cA, afm_rdy)
                    for hc in range(HEADS):
                        P.em("tensor", lambda e, o=pb[bk][:, 0:256], l=wsl2[:, par, 128 * hc:128 * (hc + 1)], r=afm[:, hc, :], st=(hc == 0), sp=(hc == DC - 1): e.matmul(o, l, r, start=st, stop=sp))
                    tex = drain_t()
                    r_wsl[par].freed_by(cT, tex)
                    P.wait("vector", cT, tex)
                    P.em("vector", lambda e, o=x[:, dc, blk0:blk0 + 256], i=pb[bk][:, 0:256], h=cV.h: e.scalar_tensor_tensor(o, i, 1.0, o, OP.mult, OP.add).then_inc(h, 1))
                    vex = cV.inc()
                    banks[bk].freed_by(cV, vex)
                if b == 4:
                    pass
                r_afm.freed_by(cT, cT.n)
                r_kring[sp_].freed_by(cT, cT.n)
                r_vring[sp_].freed_by(cT, cT.n)

    t_iden = dma(iden[:], iden_d, cIn)
    t_onc = dma(onesc[:], onesc_d, cIn)
    t_onr = dma(onesr[:], onesr_d, cIn)
    t_cbl = dma(cblk[:], cblk_d, cIn)
    P.wait("scalar", cIn, t_cbl)
    P.wait("tensor", cIn, t_cbl)

    for it in range(10):
        par = it % 2
        r_xtm[par].use(P, "sync")
        xt = dma(xtm2[:, par, :], xin[128 * it:128 * (it + 1), :], cIn)
        P.wait("tensor", cIn, xt)
        xr = xtm2[:, par, :].rearrange("p (c t u) -> p c t u", t=128, u=2)
        for half in (0, 1):
            for qq in range(8):
                chh = half * 8 + qq
                banks[qq].use(P, "tensor")
                P.em("tensor", lambda e, o=pb[qq][:, 0:128], i=xr[:, chh >> 1, :, chh & 1]: e.transpose(o, i, iden[:]))
            tex = drain_t()
            if half == 1:
                r_xtm[par].freed_by(cT, tex)
            P.wait("scalar", cT, tex)
            for qq in range(8):
                chh = half * 8 + qq
                P.em("scalar", lambda e, o=x[:, chh, 128 * it:128 * (it + 1)], i=pb[qq][:, 0:128], h=cA.h: e.copy(o, i).then_inc(h, 1))
                aex = cA.inc()
                banks[qq].freed_by(cA, aex)

    for wi, (wsrc, wdst, slab) in enumerate(((wq_d, scrAq, True), (wk_d, scrAk, True), (wv_d, scrAv, False), (wo_d, scrAo, True))):
        for fc in range(DC):
            par = (wi * DC + fc) % 2
            r_abuf[par].use(P, "gpsimd")
            ca = dma(abuf2[:, par, :], wsrc[128 * fc:128 * (fc + 1), :], cCa, eng="gpsimd")
            P.wait("sync", cCa, ca)
            if slab:
                wr = dma(wdst[:, :, fc, :].transpose([1, 0, 2]), abuf2[:, par, :].rearrange("p (o d) -> p o d", d=128), cWr)
            else:
                wr = dma(wdst[fc], abuf2[:, par, :], cWr)
            r_abuf[par].freed_by(cWr, wr)

    barrier()

    emit_ln(T3)
    barrier()
    tap("t_ln1", lnb[:].rearrange("p a b -> p (a b)"))

    emit_ffn(True, T3)
    barrier()
    tap("t_x1", x[:].rearrange("p a b -> p (a b)"))

    emit_ln(T3)
    barrier()
    tap("t_ln2", lnb[:].rearrange("p a b -> p (a b)"))

    emit_attn()
    barrier()
    tap("t_x2", x[:].rearrange("p a b -> p (a b)"))

    emit_ln(T2)
    barrier()
    tap("t_ln3", lnb[:].rearrange("p a b -> p (a b)"))

    emit_ffn(False, T2)
    barrier()

    for ot in range(8):
        par = ot % 2
        t0 = 256 + 128 * ot
        r_otm[par].use(P, "vector")
        xw = xtm2[:, par, :].rearrange("p (c t u) -> p c t u", t=128, u=2)
        for half in (0, 1):
            for qq in range(8):
                chh = half * 8 + qq
                banks[qq].use(P, "tensor")
                P.em("tensor", lambda e, o=pb[qq][:, 0:128], i=x[:, chh, t0:t0 + 128]: e.transpose(o, i, iden[:]))
            tex = drain_t()
            P.wait("vector", cT, tex)
            for qq in range(8):
                chh = half * 8 + qq
                P.em("vector", lambda e, o=xw[:, chh >> 1, :, chh & 1], i=pb[qq][:, 0:128], h=cV.h: e.tensor_copy(o, i).then_inc(h, 1))
                vex = cV.inc()
                banks[qq].freed_by(cV, vex)
        P.wait("sync", cV, vex)
        ot_t = dma(y_d[128 * ot:128 * (ot + 1), :], xtm2[:, par, :], cOut)
        r_otm[par].freed_by(cOut, ot_t)
    P.wait("sync", cOut, cOut.n)

    with nc.Block() as blk:
        for eng in ENGINES:
            ops = P.ops[eng]

            def body(e, ops=ops):
                for fn in ops:
                    fn(e)

            getattr(blk, eng)(body)

    for cm in sem_hold:
        cm.__exit__(None, None, None)
    return nc


_NC_CACHE = [None]


def _mkperm(nblk):
    o = np.arange(nblk)[:, None]
    p = np.arange(128)[None, :]
    return (256 * (o >> 1) + 2 * p + (o & 1)).reshape(-1)


def _host_inputs(inputs):
    xfull = np.asarray(inputs["L0_attn_hidden_states"])
    PD = _mkperm(16)
    PM = _mkperm(128)
    gpp = np.asarray(inputs["L0_ffn_gate_W_packed"])[PM]
    upp = np.asarray(inputs["L0_ffn_up_W_packed"])[PM]
    dpp = np.asarray(inputs["L0_ffn_down_W_packed"])[PD]
    shared = dict(
        wq=np.ascontiguousarray(np.asarray(inputs["L0_attn_Wq"])[PD]),
        wk=np.ascontiguousarray(np.asarray(inputs["L0_attn_Wk"])[PD]),
        wv=np.ascontiguousarray(np.asarray(inputs["L0_attn_Wv"])[PD]),
        wo=np.ascontiguousarray(np.asarray(inputs["L0_attn_Wo"])[:, PD]),
        gp=np.ascontiguousarray(gpp.T.reshape(8, 128, 128, 128).transpose(2, 1, 0, 3).reshape(128, 128, 1024)),
        gs=np.ascontiguousarray(np.asarray(inputs["L0_ffn_gate_scales"])[PM].reshape(128, 128, 8, 8).transpose(0, 3, 2, 1)),
        up=np.ascontiguousarray(upp.T.reshape(8, 128, 128, 128).transpose(2, 1, 0, 3).reshape(128, 128, 1024)),
        us=np.ascontiguousarray(np.asarray(inputs["L0_ffn_up_scales"])[PM].reshape(128, 128, 8, 8).transpose(0, 3, 2, 1)),
        dp=np.ascontiguousarray(dpp.T.reshape(64, 128, 2048)),
        ds=np.ascontiguousarray(np.asarray(inputs["L0_ffn_down_scales"])[PD].reshape(16, 128, 64, 8).transpose(2, 3, 0, 1)),
    )
    inv = 1.0 / (10000.0 ** (np.arange(0, 128, 2, dtype=np.float64) / 128.0))
    invf = np.concatenate([inv, inv])
    sgn = np.ones(128)
    sgn[:64] = -1.0

    maps = []
    for core in range(8):
        bb, qd = divmod(core, 4)
        own = xfull[bb, qd * 1024:(qd + 1) * 1024]
        if qd > 0:
            halo = xfull[bb, qd * 1024 - 256:qd * 1024]
            pos = np.arange(qd * 1024 - 256, qd * 1024 + 1024, dtype=np.float64)
        else:
            halo = xfull[bb, 0:256]
            pos = np.concatenate([np.arange(256), np.arange(1024)]).astype(np.float64)
        xc = np.ascontiguousarray(np.concatenate([halo, own], axis=0), dtype=np.float32)
        ang = invf[:, None] * pos[None, :]
        c64 = np.cos(ang)
        s64 = np.sin(ang) * sgn[:, None]
        chv = c64.astype(np.float32)
        clv = (c64 - chv).astype(np.float32)
        shv = s64.astype(np.float32)
        slv = (s64 - shv).astype(np.float32)
        m = np.zeros((128, 4, 2, 512), np.float32)
        pidx = np.arange(128)
        for c in range(4):
            gchunk = qd * 4 + c
            if gchunk == 0:
                m[:, c, :, 0:256] = -30000.0
            for qh in range(2):
                qrow = qh * 128 + pidx
                kk = np.arange(256)
                bad = qrow[:, None] < kk[None, :]
                m[:, c, qh, 256:512] = np.where(bad, -30000.0, 0.0)
        maps.append(dict(
            xin=xc,
            cosh=np.ascontiguousarray(chv), cosl=np.ascontiguousarray(clv),
            sinh=np.ascontiguousarray(shv), sinl=np.ascontiguousarray(slv),
            msk=np.ascontiguousarray(m.astype(ml_dtypes.bfloat16)),
            **shared,
        ))
    return maps


def kernel(**inputs):
    if _NC_CACHE[0] is None:
        _NC_CACHE[0] = build_nc()
    nc = _NC_CACHE[0]
    maps = _host_inputs(inputs)
    res = run_bass_kernel_spmd(nc, maps, core_ids=list(range(8)))
    out = np.empty((2, 4096, 2048), np.float32)
    for core in range(8):
        bb, qd = divmod(core, 4)
        out[bb, qd * 1024:(qd + 1) * 1024] = res.results[core]["y"]
    if _DEBUG:
        kernel.last_results = res
    return out


if __name__ == "__main__":
    nc = build_nc()
    print("built ok, instructions:", nc.n_instructions if hasattr(nc, "n_instructions") else "?")


# version 09
# speedup vs baseline: 1.0460x; 1.0460x over previous
import math
import os

import numpy as np
import ml_dtypes
import concourse.bass as bass
import concourse.mybir as mybir
from concourse.bass_utils import run_bass_kernel_spmd

F32 = mybir.dt.float32
F16 = mybir.dt.float16
BF16 = mybir.dt.bfloat16
I32 = mybir.dt.int32
I16 = mybir.dt.int16
AF = mybir.ActivationFunctionType
OP = mybir.AluOpType

D = 2048
DC = 16
MLP = 16384
NB = 128
GRP = 4
NG = NB // GRP
HEADS = 16
TOT = 1280
OWN = 1024
T3 = [(0, 512), (512, 512), (1024, 256)]
T2 = [(256, 512), (768, 512)]
RS = 1.0 / math.sqrt(128.0)

ENGINES = ("sync", "vector", "scalar", "tensor", "gpsimd")
_DEBUG = int(os.environ.get("K_DEBUG", "0"))


class Ctr:
    __slots__ = ("h", "n", "qeng", "maxw", "gated")

    def __init__(self, h, qeng=None):
        self.h = h
        self.n = 0
        self.qeng = qeng
        self.maxw = 0
        self.gated = 0

    def inc(self, k=1):
        self.n += k
        return self.n


class Prog:
    def __init__(self):
        self.ops = {e: [] for e in ENGINES}
        self.wmax = {}

    def em(self, eng, fn):
        self.ops[eng].append(fn)

    def wait(self, eng, ctr, val):
        if val <= 0:
            return
        if val > ctr.maxw:
            ctr.maxw = val
        if eng == ctr.qeng and val > ctr.gated:
            ctr.gated = val
        key = (eng, id(ctr))
        if self.wmax.get(key, -1) >= val:
            return
        self.wmax[key] = val
        h = ctr.h
        self.em(eng, lambda e, h=h, v=val: e.wait_ge(h, v))


class Res:
    __slots__ = ("deps",)

    def __init__(self):
        self.deps = []

    def use(self, P, eng):
        for (c, v) in self.deps:
            P.wait(eng, c, v)
        self.deps = []

    def freed_by(self, ctr, v):
        if ctr.qeng is not None and v > ctr.maxw:
            ctr.maxw = v
        self.deps.append((ctr, v))


def build_nc():
    nc = bass.Bass("TRN2", target_bir_lowering=False)

    xin = nc.dram_tensor("xin", [TOT, D], F32, kind="ExternalInput").ap()
    wq_d = nc.dram_tensor("wq", [D, D], F32, kind="ExternalInput").ap()
    wk_d = nc.dram_tensor("wk", [D, D], F32, kind="ExternalInput").ap()
    wv_d = nc.dram_tensor("wv", [D, D], F32, kind="ExternalInput").ap()
    wo_d = nc.dram_tensor("wo", [D, D], F32, kind="ExternalInput").ap()
    gp_d = nc.dram_tensor("gp", [NB, 128, 1024], I32, kind="ExternalInput").ap()
    gs_d = nc.dram_tensor("gs", [NB, 8, 8, 128], F32, kind="ExternalInput").ap()
    up_d = nc.dram_tensor("up", [NB, 128, 1024], I32, kind="ExternalInput").ap()
    us_d = nc.dram_tensor("us", [NB, 8, 8, 128], F32, kind="ExternalInput").ap()
    dp_d = nc.dram_tensor("dp", [64, 128, 2048], I32, kind="ExternalInput").ap()
    ds_d = nc.dram_tensor("ds", [64, 8, 16, 128], F32, kind="ExternalInput").ap()
    ch_d = nc.dram_tensor("cosh", [128, TOT], F32, kind="ExternalInput").ap()
    cl_d = nc.dram_tensor("cosl", [128, TOT], F32, kind="ExternalInput").ap()
    sh_d = nc.dram_tensor("sinh", [128, TOT], F32, kind="ExternalInput").ap()
    sl_d = nc.dram_tensor("sinl", [128, TOT], F32, kind="ExternalInput").ap()
    msk_d = nc.dram_tensor("msk", [128, 4, 2, 512], BF16, kind="ExternalInput").ap()

    y_d = nc.dram_tensor("y", [OWN, D], F32, kind="ExternalOutput").ap()
    taps = {}
    if _DEBUG:
        for nm, dt in (("t_ln1", BF16), ("t_x1", F32), ("t_ln2", BF16), ("t_x2", F32), ("t_ln3", BF16)):
            taps[nm] = nc.dram_tensor(nm, [128, DC * TOT], dt, kind="ExternalOutput").ap()

    scrG = nc.dram_tensor("scrG", [NB, 128, 2048], BF16, kind="Internal").ap()
    scrU = nc.dram_tensor("scrU", [NB, 128, 2048], BF16, kind="Internal").ap()
    scrD = nc.dram_tensor("scrD", [NB, 128, 2048], BF16, kind="Internal").ap()
    scrAq = nc.dram_tensor("scrAq", [16, 128, 16, 128], BF16, kind="Internal").ap()
    scrAk = nc.dram_tensor("scrAk", [16, 128, 16, 128], BF16, kind="Internal").ap()
    scrAv = nc.dram_tensor("scrAv", [16, 128, D], BF16, kind="Internal").ap()
    scrAo = nc.dram_tensor("scrAo", [16, 128, 16, 128], BF16, kind="Internal").ap()

    iden_d = nc.inline_tensor(np.eye(128, dtype=np.float32), name="idn").ap()
    onesc_d = nc.inline_tensor(np.ones((128, 1), np.float32), name="onc").ap()
    onesr_d = nc.inline_tensor(np.ones((1, 128), np.float32), name="onr").ap()
    cvals = np.zeros((128, 8), np.float32)
    cvals[:, 0] = -4.0
    cvals[:, 1] = -12.0
    cvals[:, 2] = 0.5
    cvals[:, 3] = -0.125
    cvals[:, 4] = 1e-5
    cvals[:, 5] = 2.0
    cblk_d = nc.inline_tensor(cvals, name="cbl").ap()

    TOPA = 229344
    FLOOR = 16500
    cur = [TOPA]
    szof = {F32: 4, F16: 2, BF16: 2, I32: 4, I16: 2}

    def alloc(name, shape, dt):
        nbytes = math.prod(shape[1:]) * szof[dt]
        nbytes = (nbytes + 127) // 128 * 128
        cur[0] -= nbytes
        assert cur[0] >= FLOOR, f"SBUF overflow at {name}: {cur[0]}"
        return nc.alloc_sbuf_tensor_at(name, shape, dt, offset=cur[0])

    x = alloc("x", [128, DC, TOT], F32)
    lnb = alloc("lnb", [128, DC, TOT], BF16)
    iden = alloc("iden", [128, 128], F32)
    onesc = alloc("onesc", [128, 1], F32)
    onesr = alloc("onesr", [1, 128], F32)
    cblk = alloc("cblk", [128, 8], F32)

    AR_TOP = cur[0]

    def arena():
        cur[0] = AR_TOP

    arena()
    pk_gu = alloc("pk_gu", [128, 3, 1024], I16)
    sc_gu = alloc("sc_gu", [128, 3, 1024], F32)
    w_gu = alloc("w_gu", [128, 5, 2, 1024], BF16)
    pk_dn = alloc("pk_dn", [128, 2, 1024], I16)
    sc_dn = alloc("sc_dn", [128, 1, 1024], F32)
    w_dn = alloc("w_dn", [128, 3, 2, 1024], BF16)
    c7b = alloc("c7b", [128, 2, 512], I16)
    s8b = alloc("s8b", [128, 2, 512], I16)
    r1b = alloc("r1b", [128, 4, 512], F16)
    r2b = alloc("r2b", [128, 4, 512], F16)
    sgb = alloc("sgb", [128, 4, 512], F16)
    inter = alloc("inter", [128, GRP, TOT], BF16)
    gel2 = alloc("gel2", [128, 2, 512], BF16)

    arena()
    kring = alloc("kring", [128, 2, HEADS, 256], BF16)
    vring = alloc("vring", [128, 2, 2, D], BF16)
    qbuf = alloc("qbuf", [128, HEADS, 256], BF16)
    afm = alloc("afm", [128, HEADS, 256], BF16)
    wsl2 = alloc("wsl2", [128, 2, 2048], BF16)
    cs4 = alloc("cs4", [128, 4, 256], F32)
    mskb = alloc("mskb", [128, 2, 512], BF16)
    smb = alloc("smb", [128, 512], F32)
    pnraw = alloc("pnraw", [128, 1, 512], BF16)
    pn2b = alloc("pn2b", [128, 1, 512], BF16)
    pTb = alloc("pTb", [128, 4, 256], BF16)
    tA2 = alloc("tA2", [128, 2, 256], F32)
    tB2 = alloc("tB2", [128, 2, 256], F32)
    rmx = alloc("rmx", [128, 1], F32)
    rsm = alloc("rsm", [128, 1], F32)
    rin = alloc("rin", [128, 1], F32)

    arena()
    xtm2 = alloc("xtm2", [128, 2, 2048], F32)
    abuf2 = alloc("abuf2", [128, 2, 2048], BF16)

    arena()
    stA = alloc("stA", [1, 512], F32)
    stB = alloc("stB", [1, 512], F32)
    stC = alloc("stC", [1, 512], F32)
    stR = alloc("stR", [1, 512], F32)
    sqb = alloc("sqb", [128, 512], F32)
    t1b = alloc("t1b", [128, 512], F32)

    pb = [nc.alloc_psum_tensor("pb%d" % i, [128, 512], F32) for i in range(8)]

    sem_hold = []

    def mksem(name):
        cm = nc.semaphore(name)
        h = cm.__enter__()
        sem_hold.append(cm)
        return Ctr(h)

    cIn = mksem("cIn")
    cPk = mksem("cPk")
    cTr = mksem("cTr")
    cWr = mksem("cWr")
    cCa = mksem("cCa")
    cOut = mksem("cOut")
    cV = mksem("cV")
    cA = mksem("cA")
    cP = mksem("cP")
    cT = mksem("cT")
    cPh = mksem("cPh")
    cTr2 = mksem("cTr2")
    for _c in (cIn, cTr, cTr2, cWr, cOut):
        _c.qeng = "sync"
    cPk.qeng = "scalar"
    cCa.qeng = "gpsimd"

    P = Prog()
    phase = [0]

    banks = [Res() for _ in range(8)]
    r_pkgu = [Res() for _ in range(3)]
    r_scgu = [Res() for _ in range(3)]
    r_wgu = [Res() for _ in range(5)]
    r_pkdn = [Res(), Res()]
    r_scdn = [Res()]
    r_wdn = [Res() for _ in range(3)]
    r_c7 = [Res(), Res()]
    r_s8 = [Res(), Res()]
    r_r1 = [Res() for _ in range(4)]
    r_r2 = [Res() for _ in range(4)]
    r_sg = [Res() for _ in range(4)]
    r_gel = [Res(), Res()]
    r_inter = Res()
    r_sq, r_t1 = Res(), Res()
    r_stA, r_stB, r_stC, r_stR = Res(), Res(), Res(), Res()
    r_xtm = [Res(), Res()]
    r_abuf = [Res(), Res()]
    r_wsl = [Res(), Res()]
    r_cs = Res()
    r_msk = Res()
    r_smb = Res()
    r_pnr = Res()
    r_pn2 = Res()
    r_pT = Res()
    r_tA = [Res(), Res()]
    r_tB = [Res(), Res()]
    r_kring = [Res(), Res()]
    r_vring = [Res(), Res()]
    r_qbuf = Res()
    r_afm = Res()
    r_rmx = Res()
    r_rsm = Res()
    r_rin = Res()
    r_otm = [Res(), Res()]

    def dma(out, in_, ctr, eng="sync"):
        if ctr.maxw > ctr.gated:
            P.wait(ctr.qeng, ctr, ctr.maxw)
        P.em(eng, lambda e, o=out, i=in_, h=ctr.h: e.dma_start(out=o, in_=i).then_inc(h, 16))
        t = ctr.inc(16)
        ctr.maxw = max(ctr.maxw, t)
        return t

    def dma_tr(out, in_, ctr):
        if ctr.maxw > ctr.gated:
            P.wait(ctr.qeng, ctr, ctr.maxw)
        P.em("sync", lambda e, o=out, i=in_, h=ctr.h: e.dma_start(out=o, in_=i, transpose=True).then_inc(h, 16))
        return ctr.inc(16)

    def drain_t():
        P.em("tensor", lambda e, h=cT.h: e.drain().then_inc(h, 1))
        return cT.inc()

    def barrier():
        for c in (cIn, cTr, cTr2, cWr, cOut):
            P.wait("sync", c, c.n)
        P.wait("scalar", cPk, cPk.n)
        P.wait("gpsimd", cCa, cCa.n)
        phase[0] += 1
        tgt = 5 * phase[0]
        for e in ENGINES:
            P.em(e, lambda en, h=cPh.h: en.nop().then_inc(h, 1))
        for e in ENGINES:
            P.wait(e, cPh, tgt)

    def tap(name, src):
        if name in taps:
            t = dma(taps[name], src, cOut)
            P.wait("sync", cOut, t)

    tmp_flip = [0]

    def emit_dequant(pk_ap, sc_ap, w_ap, r_pk, r_sc, r_w, pk_rdy, sc_rdy):
        subs = []
        for nib in (0, 1):
            for hh in (0, 1):
                subs.append((nib, hh, tmp_flip[0] & 1, tmp_flip[0] & 3))
                tmp_flip[0] += 1
        vex = [0] * 4
        aex = [0] * 4
        v2x = [0] * 4
        state = {"first_w": True}
        wsub = [0] * 4

        def ext(i):
            nib, hh, p2, p4 = subs[i]
            src = pk_ap[:, 512 * hh:512 * (hh + 1)]
            c7 = c7b[:, p2, :]
            s8 = s8b[:, p2, :]
            r_c7[p2].use(P, "vector")
            r_s8[p2].use(P, "vector")
            P.wait("vector", cCa, pk_rdy)
            if nib == 0:
                P.em("vector", lambda e, o=c7, i=src: e.tensor_scalar(o, i, 7, None, OP.bitwise_and))
                P.em("vector", lambda e, o=s8, i=src, h=cV.h: e.tensor_scalar(o, i, 8, None, OP.bitwise_and).then_inc(h, 1))
            else:
                P.em("vector", lambda e, o=c7, i=src: e.tensor_scalar(o, i, 4, 7, OP.logical_shift_right, OP.bitwise_and))
                P.em("vector", lambda e, o=s8, i=src, h=cV.h: e.tensor_scalar(o, i, 4, 8, OP.logical_shift_right, OP.bitwise_and).then_inc(h, 1))
            vex[i] = cV.inc()
            if i == 3:
                r_pk.freed_by(cV, vex[i])

        def acts(i):
            nib, hh, p2, p4 = subs[i]
            c7 = c7b[:, p2, :]
            s8 = s8b[:, p2, :]
            r1 = r1b[:, p4, :]
            r2 = r2b[:, p4, :]
            sg = sgb[:, p4, :]
            P.wait("scalar", cV, vex[i])
            r_r1[p4].use(P, "scalar")
            r_r2[p4].use(P, "scalar")
            r_sg[p4].use(P, "scalar")
            P.em("scalar", lambda e, o=r1, i=c7: e.activation(o, i, AF.Relu, bias=cblk[:, 0:1], scale=1.0))
            P.em("scalar", lambda e, o=r2, i=c7: e.activation(o, i, AF.Relu, bias=cblk[:, 1:2], scale=cblk[:, 5:6]))
            P.em("scalar", lambda e, o=sg, i=s8, h=cA.h: e.activation(o, i, AF.Identity, bias=cblk[:, 2:3], scale=cblk[:, 3:4]).then_inc(h, 1))
            aex[i] = cA.inc()
            r_s8[p2].freed_by(cA, aex[i])

        def vmul(i):
            nib, hh, p2, p4 = subs[i]
            c7 = c7b[:, p2, :]
            r1 = r1b[:, p4, :]
            r2 = r2b[:, p4, :]
            sg = sgb[:, p4, :]
            P.wait("vector", cA, aex[i])
            P.em("vector", lambda e, o=r1, i=c7, b=r1: e.scalar_tensor_tensor(o, i, 1.0, b, OP.mult, OP.add))
            P.em("vector", lambda e, o=r2, a=r1, b=r2: e.tensor_tensor(o, a, b, OP.add))
            P.em("vector", lambda e, o=sg, a=r2, h=cV.h: e.tensor_tensor(o, a, o, OP.mult).then_inc(h, 1))
            v2x[i] = cV.inc()
            r_c7[p2].freed_by(cV, v2x[i])
            r_r1[p4].freed_by(cV, v2x[i])
            r_r2[p4].freed_by(cV, v2x[i])

        def gfin(i):
            nib, hh, p2, p4 = subs[i]
            sg = sgb[:, p4, :]
            scs = sc_ap[:, 512 * hh:512 * (hh + 1)]
            dst = w_ap[:, nib, 512 * hh:512 * (hh + 1)]
            P.wait("gpsimd", cV, v2x[i])
            P.wait("gpsimd", cPk, sc_rdy)
            if state["first_w"]:
                r_w.use(P, "gpsimd")
                state["first_w"] = False
            P.em("gpsimd", lambda e, o=dst, a=sg, b=scs, h=cP.h: e.tensor_tensor(o, a, b, OP.mult).then_inc(h, 1))
            wsub[i] = cP.inc()
            r_sg[p4].freed_by(cP, wsub[i])
            if i == 3:
                r_sc.freed_by(cP, wsub[i])

        ext(0)
        acts(0)
        ext(1)
        acts(1)
        for i in range(4):
            vmul(i)
            gfin(i)
            if i + 2 < 4:
                ext(i + 2)
                acts(i + 2)
        return [(cP, wsub[i]) for i in range(4)]

    def emit_ln(tiles):
        for (t0, T) in tiles:
            for ch in range(DC):
                xs = x[:, ch, t0:t0 + T]
                r_sq.use(P, "scalar")
                P.em("scalar", lambda e, o=sqb[:, 0:T], i=xs, h=cA.h: e.activation(o, i, AF.Square, bias=0.0, scale=1.0).then_inc(h, 1))
                aex = cA.inc()
                if ch == 0:
                    banks[6].use(P, "tensor")
                    banks[7].use(P, "tensor")
                P.em("tensor", lambda e, o=pb[6][0:1, 0:T], i=xs, st=(ch == 0), sp=(ch == DC - 1): e.matmul(o, onesc[:], i, start=st, stop=sp))
                P.wait("tensor", cA, aex)
                P.em("tensor", lambda e, o=pb[7][0:1, 0:T], i=sqb[:, 0:T], st=(ch == 0), sp=(ch == DC - 1), h=cT.h: e.matmul(o, onesc[:], i, start=st, stop=sp).then_inc(h, 1))
                r_sq.freed_by(cT, cT.inc())
            tex = drain_t()
            P.wait("vector", cT, tex)
            r_stA.use(P, "vector")
            r_stC.use(P, "vector")
            r_stB.use(P, "vector")
            P.em("vector", lambda e, o=stA[0:1, 0:T], i=pb[6][0:1, 0:T], h=cV.h: e.tensor_scalar(o, i, -1.0 / 2048.0, None, OP.mult).then_inc(h, 1))
            P.wait("vector", cV, cV.inc())
            P.em("vector", lambda e, o=stC[0:1, 0:T], a=stA[0:1, 0:T], h=cV.h: e.tensor_tensor(o, a, a, OP.mult).then_inc(h, 1))
            P.wait("vector", cV, cV.inc())
            P.em("vector", lambda e, o=stB[0:1, 0:T], i=pb[7][0:1, 0:T], c=stC[0:1, 0:T], h=cV.h: e.scalar_tensor_tensor(o, i, 1.0 / 2048.0, c, OP.mult, OP.subtract).then_inc(h, 1))
            vex = cV.inc()
            banks[6].freed_by(cV, vex)
            banks[7].freed_by(cV, vex)
            P.wait("scalar", cV, vex)
            r_stC.use(P, "scalar")
            P.em("scalar", lambda e, o=stC[0:1, 0:T], i=stB[0:1, 0:T], h=cA.h: e.activation(o, i, AF.Sqrt, bias=cblk[0:1, 4:5], scale=1.0).then_inc(h, 1))
            aex = cA.inc()
            r_stB.freed_by(cA, aex)
            P.wait("vector", cA, aex)
            r_stR.use(P, "vector")
            P.em("vector", lambda e, o=stR[0:1, 0:T], i=stC[0:1, 0:T], h=cV.h: e.reciprocal(o, i).then_inc(h, 1))
            vex = cV.inc()
            r_stC.freed_by(cV, vex)
            P.wait("gpsimd", cV, vex)
            P.em("gpsimd", lambda e, o=stA[0:1, 0:T], b=stR[0:1, 0:T], h=cP.h: e.tensor_tensor(o, o, b, OP.mult).then_inc(h, 1))
            pex = cP.inc()
            banks[4].use(P, "tensor")
            banks[5].use(P, "tensor")
            P.wait("tensor", cV, vex)
            P.em("tensor", lambda e, o=pb[4][:, 0:T], i=stR[0:1, 0:T]: e.matmul(o, onesr[:], i, start=True, stop=True))
            P.wait("tensor", cP, pex)
            P.em("tensor", lambda e, o=pb[5][:, 0:T], i=stA[0:1, 0:T]: e.matmul(o, onesr[:], i, start=True, stop=True))
            tex = drain_t()
            r_stR.freed_by(cT, tex)
            r_stA.freed_by(cT, tex)
            P.wait("vector", cT, tex)
            for ch in range(DC):
                xs = x[:, ch, t0:t0 + T]
                ls = lnb[:, ch, t0:t0 + T]
                r_t1.use(P, "vector")
                P.em("vector", lambda e, o=t1b[:, 0:T], a=xs, b=pb[4][:, 0:T], h=cV.h: e.tensor_tensor(o, a, b, OP.mult).then_inc(h, 1))
                P.wait("vector", cV, cV.inc())
                P.em("vector", lambda e, o=ls, a=t1b[:, 0:T], b=pb[5][:, 0:T], h=cV.h: e.tensor_tensor(o, a, b, OP.add).then_inc(h, 1))
                vex = cV.inc()
                r_t1.freed_by(cV, vex)
            banks[4].freed_by(cV, vex)
            banks[5].freed_by(cV, vex)

    gu_slot = [0]
    dn_slot = [0]
    dpk_slot = [0]
    mmctr = [0]

    def bcast_sc(src3):
        a, b = src3.shape[1], src3.shape[2]
        return src3.rearrange("q a b -> q (a b)").unsqueeze(1).broadcast_to([8, 16, a * b])

    def emit_ffn(deq, tiles):
        nt = len(tiles)
        for g in range(NG):
            r_inter.use(P, "vector")
            inter_rdy = [0] * nt
            for j in range(GRP):
                nb = g * GRP + j
                wrdy = []
                for (pk_hbm, sc_hbm, scr) in ((gp_d, gs_d, scrG), (up_d, us_d, scrU)):
                    slp = gu_slot[0] % 3
                    sl = gu_slot[0] % 5
                    gu_slot[0] += 1
                    if deq:
                        r_pkgu[slp].use(P, "gpsimd")
                        pk_t = dma(pk_gu[:, slp, :], pk_hbm[nb], cCa, eng="gpsimd")
                        r_scgu[slp].use(P, "scalar")
                        sc_t = dma(sc_gu[:, slp, :], bcast_sc(sc_hbm[nb]), cPk, eng="scalar")
                        wsubs = emit_dequant(pk_gu[:, slp, :], sc_gu[:, slp, :], w_gu[:, sl],
                                             r_pkgu[slp], r_scgu[slp], r_wgu[sl], pk_t, sc_t)
                        P.wait("sync", cP, wsubs[3][1])
                        st_t = dma(scr[nb], w_gu[:, sl, :, :], cWr)
                        r_wgu[sl].freed_by(cWr, st_t)
                        wrdy.append((sl, wsubs))
                    else:
                        r_wgu[sl].use(P, "sync")
                        w_t = dma(w_gu[:, sl, :, :], scr[nb], cIn)
                        wrdy.append((sl, [(cIn, w_t)] * 4))
                (slg, g_rdy), (slu, u_rdy) = wrdy
                for ti, (t0, T) in enumerate(tiles):
                    bg = mmctr[0] % 2
                    mmctr[0] += 1
                    bu = 2 + bg
                    for (bk, slw, rdy) in ((bg, slg, g_rdy), (bu, slu, u_rdy)):
                        banks[bk].use(P, "tensor")
                        nmm = 0
                        for si in range(4):
                            P.wait("tensor", rdy[si][0], rdy[si][1])
                            nib, hh = si >> 1, si & 1
                            for c in range(4 * hh, 4 * hh + 4):
                                ck = 2 * c + nib
                                P.em("tensor", lambda e, o=pb[bk][:, 0:T], l=w_gu[:, slw, nib, c * 128:c * 128 + 128], r=lnb[:, ck, t0:t0 + T], st=(nmm == 0), sp=(nmm == DC - 1): e.matmul(o, l, r, start=st, stop=sp))
                                nmm += 1
                    tex = drain_t()
                    if ti == nt - 1:
                        r_wgu[slg].freed_by(cT, tex)
                        r_wgu[slu].freed_by(cT, tex)
                    P.wait("scalar", cT, tex)
                    r_gel[ti % 2].use(P, "scalar")
                    P.em("scalar", lambda e, o=gel2[:, ti % 2, 0:T], i=pb[bg][:, 0:T], h=cA.h: e.activation(o, i, AF.Gelu_apprx_tanh, bias=0.0, scale=1.0).then_inc(h, 1))
                    aex = cA.inc()
                    banks[bg].freed_by(cA, aex)
                    P.wait("vector", cA, aex)
                    P.wait("vector", cT, tex)
                    P.em("vector", lambda e, o=inter[:, j, t0:t0 + T], a=gel2[:, ti % 2, 0:T], b=pb[bu][:, 0:T], h=cV.h: e.tensor_tensor(o, a, b, OP.mult).then_inc(h, 1))
                    vex = cV.inc()
                    banks[bu].freed_by(cV, vex)
                    r_gel[ti % 2].freed_by(cV, vex)
                    inter_rdy[ti] = vex
            last_tex = 0
            for h in (0, 1):
                units = []
                for c2 in (2 * g, 2 * g + 1):
                    dsl = dn_slot[0] % 3
                    dn_slot[0] += 1
                    if deq:
                        psl = dpk_slot[0] % 2
                        dpk_slot[0] += 1
                        r_pkdn[psl].use(P, "gpsimd")
                        pk_t = dma(pk_dn[:, psl, :], dp_d[c2, :, 1024 * h:1024 * (h + 1)], cCa, eng="gpsimd")
                        r_scdn[0].use(P, "scalar")
                        sc_t = dma(sc_dn[:, 0, :], bcast_sc(ds_d[c2, :, 8 * h:8 * (h + 1), :]), cPk, eng="scalar")
                        wsubs = emit_dequant(pk_dn[:, psl, :], sc_dn[:, 0, :], w_dn[:, dsl],
                                             r_pkdn[psl], r_scdn[0], r_wdn[dsl], pk_t, sc_t)
                        P.wait("sync", cP, wsubs[3][1])
                        st_t = dma(scrD[c2 * 2 + h], w_dn[:, dsl, :, :], cWr)
                        r_wdn[dsl].freed_by(cWr, st_t)
                        units.append((dsl, wsubs))
                    else:
                        r_wdn[dsl].use(P, "sync")
                        w_t = dma(w_dn[:, dsl, :, :], scrD[c2 * 2 + h], cIn)
                        units.append((dsl, [(cIn, w_t)] * 4))
                for ti, (t0, T) in enumerate(tiles):
                    for batch in range(4):
                        bis = []
                        for q in (0, 1):
                            ck = 8 * h + 2 * batch + q
                            cki = 2 * batch + q
                            bi = 4 + 2 * (batch & 1) + q
                            bis.append((ck, bi))
                            banks[bi].use(P, "tensor")
                            for jjj in range(4):
                                dsl_u, rdy = units[jjj >> 1]
                                nib = jjj & 1
                                si = nib * 2 + (1 if cki >= 4 else 0)
                                P.wait("tensor", rdy[si][0], rdy[si][1])
                                P.wait("tensor", cV, inter_rdy[ti])
                                P.em("tensor", lambda e, o=pb[bi][:, 0:T], l=w_dn[:, dsl_u, nib, cki * 128:cki * 128 + 128], r=inter[:, jjj, t0:t0 + T], st=(jjj == 0), sp=(jjj == 3): e.matmul(o, l, r, start=st, stop=sp))
                        tex = drain_t()
                        last_tex = tex
                        for (ck, bi) in bis:
                            P.wait("vector", cT, tex)
                            P.em("vector", lambda e, o=x[:, ck, t0:t0 + T], i=pb[bi][:, 0:T], h=cV.h: e.scalar_tensor_tensor(o, i, 0.5, o, OP.mult, OP.add).then_inc(h, 1))
                            vex = cV.inc()
                            banks[bi].freed_by(cV, vex)
                for (dsl_u, rdy) in units:
                    r_wdn[dsl_u].freed_by(cT, last_tex)
            r_inter.freed_by(cT, last_tex)

    rope_flip = [0]

    def emit_rope(ps, dst, tex, cs_t):
        fp = rope_flip[0] & 1
        rope_flip[0] += 1
        tA = tA2[:, fp, :]
        tB = tB2[:, fp, :]
        chs = cs4[:, 0, :]
        cls = cs4[:, 1, :]
        shs = cs4[:, 2, :]
        sls = cs4[:, 3, :]
        P.wait("vector", cT, tex)
        P.wait("vector", cIn, cs_t)
        r_tA[fp].use(P, "vector")
        r_tB[fp].use(P, "vector")
        P.em("vector", lambda e, o=tA, a=ps, b=chs: e.tensor_tensor(o, a, b, OP.mult))
        P.em("vector", lambda e, o=tB, a=ps, b=cls, h=cV.h: e.tensor_tensor(o, a, b, OP.mult).then_inc(h, 1))
        vex = cV.inc()
        P.wait("gpsimd", cV, vex)
        P.em("gpsimd", lambda e, o=tA, b=tB, h=cP.h: e.tensor_tensor(o, o, b, OP.add).then_inc(h, 1))
        pex = cP.inc()
        P.wait("vector", cP, pex)
        P.em("vector", lambda e, o=tB[0:64, :], a=ps[64:128, :], b=shs[0:64, :]: e.tensor_tensor(o, a, b, OP.mult))
        P.em("vector", lambda e, o=tB[64:128, :], a=ps[0:64, :], b=shs[64:128, :], h=cV.h: e.tensor_tensor(o, a, b, OP.mult).then_inc(h, 1))
        vex = cV.inc()
        P.wait("gpsimd", cV, vex)
        P.em("gpsimd", lambda e, o=tA, b=tB, h=cP.h: e.tensor_tensor(o, o, b, OP.add).then_inc(h, 1))
        pex = cP.inc()
        P.wait("vector", cP, pex)
        P.em("vector", lambda e, o=tB[0:64, :], a=ps[64:128, :], b=sls[0:64, :]: e.tensor_tensor(o, a, b, OP.mult))
        P.em("vector", lambda e, o=tB[64:128, :], a=ps[0:64, :], b=sls[64:128, :], h=cV.h: e.tensor_tensor(o, a, b, OP.mult).then_inc(h, 1))
        vex = cV.inc()
        P.wait("gpsimd", cV, vex)
        P.em("gpsimd", lambda e, o=dst, a=tA, b=tB, h=cP.h: e.tensor_tensor(o, a, b, OP.add).then_inc(h, 1))
        pex = cP.inc()
        r_tA[fp].freed_by(cP, pex)
        r_tB[fp].freed_by(cP, pex)
        return vex, pex

    def emit_attn():
        wsl_i = [0]

        def wsl_load(src_region):
            par = wsl_i[0] & 1
            wsl_i[0] += 1
            r_wsl[par].use(P, "sync")
            t = dma(wsl2[:, par, :], src_region, cPk)
            return par, t

        for b in range(5):
            blk0 = 256 * b
            r_cs.use(P, "sync")
            for i, src in enumerate((ch_d, cl_d, sh_d, sl_d)):
                cs_t = dma(cs4[:, i, :], src[:, blk0:blk0 + 256], cIn)
            if b >= 1:
                r_qbuf.use(P, "gpsimd")
                for oc in range(HEADS):
                    par, wt = wsl_load(scrAq[oc])
                    bk = oc % 2
                    banks[bk].use(P, "tensor")
                    P.wait("tensor", cPk, wt)
                    P.wait("tensor", cIn, cs_t)
                    for fc in range(DC):
                        P.em("tensor", lambda e, o=pb[bk][:, 0:256], l=wsl2[:, par, 128 * fc:128 * (fc + 1)], r=lnb[:, fc, blk0:blk0 + 256], st=(fc == 0), sp=(fc == DC - 1): e.matmul(o, l, r, start=st, stop=sp))
                    tex = drain_t()
                    r_wsl[par].freed_by(cT, tex)
                    vex, pex = emit_rope(pb[bk][:, 0:256], qbuf[:, oc, :], tex, cs_t)
                    banks[bk].freed_by(cV, vex)
                q_rdy = cP.n
            sl = b % 2
            r_kring[sl].use(P, "gpsimd")
            for oc in range(HEADS):
                par, wt = wsl_load(scrAk[oc])
                bk = 2 + oc % 2
                banks[bk].use(P, "tensor")
                P.wait("tensor", cPk, wt)
                P.wait("tensor", cIn, cs_t)
                for fc in range(DC):
                    P.em("tensor", lambda e, o=pb[bk][:, 0:256], l=wsl2[:, par, 128 * fc:128 * (fc + 1)], r=lnb[:, fc, blk0:blk0 + 256], st=(fc == 0), sp=(fc == DC - 1): e.matmul(o, l, r, start=st, stop=sp))
                tex = drain_t()
                r_wsl[par].freed_by(cT, tex)
                vex, pex = emit_rope(pb[bk][:, 0:256], kring[:, sl, oc, :], tex, cs_t)
                banks[bk].freed_by(cV, vex)
            k_rdy = cP.n
            r_cs.freed_by(cP, k_rdy)
            r_cs.freed_by(cV, cV.n)
            r_vring[sl].use(P, "scalar")
            for i in range(8):
                banks[i].use(P, "tensor")
            for fc in range(DC):
                par, wt = wsl_load(scrAv[fc])
                P.wait("tensor", cPk, wt)
                for kt2 in range(2):
                    for hq in range(4):
                        bi = kt2 * 4 + hq
                        last = (kt2 == 1 and hq == 3)
                        P.em("tensor", lambda e, o=pb[bi][:, 0:512], l=lnb[:, fc, blk0 + 128 * kt2:blk0 + 128 * (kt2 + 1)], r=wsl2[:, par, 512 * hq:512 * (hq + 1)], st=(fc == 0), sp=(fc == DC - 1), lt=last, h=cT.h: (e.matmul(o, l, r, start=st, stop=sp).then_inc(h, 1) if lt else e.matmul(o, l, r, start=st, stop=sp)))
                r_wsl[par].freed_by(cT, cT.inc())
            tex = drain_t()
            P.wait("scalar", cT, tex)
            for kt2 in range(2):
                for hq in range(4):
                    bi = kt2 * 4 + hq
                    P.em("scalar", lambda e, o=vring[:, sl, kt2, 512 * hq:512 * (hq + 1)], i=pb[bi][:, 0:512], h=cA.h: e.copy(o, i).then_inc(h, 1))
                    aex = cA.inc()
                    banks[bi].freed_by(cA, aex)
            v_rdy = cA.n
            if b >= 1:
                c = b - 1
                sp_ = (b - 1) % 2
                sb_ = b % 2
                r_msk.use(P, "sync")
                mt = dma(mskb[:], msk_d[:, c, :, :], cIn)
                pv_free = 0
                for h in range(HEADS):
                    for qh in range(2):
                        bk = 2 + qh
                        banks[bk].use(P, "tensor")
                        if h == 0 and qh == 0:
                            P.wait("tensor", cP, q_rdy)
                            P.wait("tensor", cP, k_rdy)
                        for si, slot in enumerate((sp_, sb_)):
                            P.em("tensor", lambda e, o=pb[bk][:, 256 * si:256 * (si + 1)], l=qbuf[:, h, 128 * qh:128 * (qh + 1)], r=kring[:, slot, h, :]: e.matmul(o, l, r, start=True, stop=True))
                    tex = drain_t()
                    for qh in range(2):
                        bk = 2 + qh
                        P.wait("vector", cT, tex)
                        P.wait("vector", cIn, mt)
                        r_smb.use(P, "vector")
                        r_rmx.use(P, "vector")
                        P.em("vector", lambda e, o=smb[:], i=pb[bk][:, 0:512], m=mskb[:, qh, :], h=cV.h: e.scalar_tensor_tensor(o, i, RS, m, OP.mult, OP.add).then_inc(h, 1))
                        P.wait("vector", cV, cV.inc())
                        P.em("vector", lambda e, o=rmx[:], i=smb[:], h=cV.h: e.reduce_max(o, i, axis=mybir.AxisListType.X, negate=True).then_inc(h, 1))
                        vex = cV.inc()
                        banks[bk].freed_by(cV, vex)
                        P.wait("scalar", cV, vex)
                        r_pnr.use(P, "scalar")
                        r_rsm.use(P, "scalar")
                        P.em("scalar", lambda e, o=pnraw[:, 0, :], i=smb[:], h=cA.h: e.activation(o, i, AF.Exp, bias=rmx[:], scale=1.0, accum_out=rsm[:]).then_inc(h, 1))
                        aex = cA.inc()
                        r_smb.freed_by(cA, aex)
                        r_rmx.freed_by(cA, aex)
                        P.wait("vector", cA, aex)
                        r_rin.use(P, "vector")
                        P.em("vector", lambda e, o=rin[:], i=rsm[:], h=cV.h: e.reciprocal(o, i).then_inc(h, 1))
                        vex = cV.inc()
                        r_rsm.freed_by(cV, vex)
                        P.wait("scalar", cV, vex)
                        r_pn2.use(P, "scalar")
                        P.em("scalar", lambda e, o=pn2b[:, 0, :], i=pnraw[:, 0, :], h=cA.h: e.activation(o, i, AF.Identity, bias=0.0, scale=rin[:]).then_inc(h, 1))
                        aex = cA.inc()
                        r_pnr.freed_by(cA, aex)
                        r_rin.freed_by(cA, aex)
                        r_pT.use(P, "sync")
                        P.wait("sync", cA, aex)
                        ctr_q = cTr if qh == 0 else cTr2
                        for ktc2 in range(4):
                            dma_tr(pTb[:, ktc2, 128 * qh:128 * (qh + 1)], pn2b[:, 0, 128 * ktc2:128 * (ktc2 + 1)], ctr_q)
                        r_pn2.freed_by(ctr_q, ctr_q.n)
                        if qh == 1:
                            ttr2 = cTr2.n
                    ttr = cTr.n
                    banks[0].use(P, "tensor")
                    P.wait("tensor", cTr, ttr)
                    P.wait("tensor", cTr2, ttr2)
                    P.wait("tensor", cA, v_rdy)
                    for ktc in range(4):
                        slot = (sp_, sp_, sb_, sb_)[ktc]
                        kt2 = ktc % 2
                        P.em("tensor", lambda e, o=pb[0][:, 0:256], l=vring[:, slot, kt2, 128 * h:128 * (h + 1)], r=pTb[:, ktc, :], st=(ktc == 0), sp=(ktc == 3): e.matmul(o, l, r, start=st, stop=sp))
                    tex = drain_t()
                    r_pT.freed_by(cT, tex)
                    if h == 0:
                        r_afm.use(P, "scalar")
                    P.wait("scalar", cT, tex)
                    P.em("scalar", lambda e, o=afm[:, h, :], i=pb[0][:, 0:256], h2=cA.h: e.copy(o, i).then_inc(h2, 1))
                    aex = cA.inc()
                    banks[0].freed_by(cA, aex)
                afm_rdy = cA.n
                r_qbuf.freed_by(cT, cT.n)
                r_msk.freed_by(cV, cV.n)
                for dc in range(DC):
                    par, wt = wsl_load(scrAo[dc])
                    bk = 1 + dc % 2
                    banks[bk].use(P, "tensor")
                    P.wait("tensor", cPk, wt)
                    P.wait("tensor", cA, afm_rdy)
                    for hc in range(HEADS):
                        P.em("tensor", lambda e, o=pb[bk][:, 0:256], l=wsl2[:, par, 128 * hc:128 * (hc + 1)], r=afm[:, hc, :], st=(hc == 0), sp=(hc == DC - 1): e.matmul(o, l, r, start=st, stop=sp))
                    tex = drain_t()
                    r_wsl[par].freed_by(cT, tex)
                    P.wait("vector", cT, tex)
                    P.em("vector", lambda e, o=x[:, dc, blk0:blk0 + 256], i=pb[bk][:, 0:256], h=cV.h: e.scalar_tensor_tensor(o, i, 1.0, o, OP.mult, OP.add).then_inc(h, 1))
                    vex = cV.inc()
                    banks[bk].freed_by(cV, vex)
                if b == 4:
                    pass
                r_afm.freed_by(cT, cT.n)
                r_kring[sp_].freed_by(cT, cT.n)
                r_vring[sp_].freed_by(cT, cT.n)

    t_iden = dma(iden[:], iden_d, cIn)
    t_onc = dma(onesc[:], onesc_d, cIn)
    t_onr = dma(onesr[:], onesr_d, cIn)
    t_cbl = dma(cblk[:], cblk_d, cIn)
    P.wait("scalar", cIn, t_cbl)
    P.wait("tensor", cIn, t_cbl)

    for it in range(10):
        par = it % 2
        r_xtm[par].use(P, "sync")
        xt = dma(xtm2[:, par, :], xin[128 * it:128 * (it + 1), :], cIn)
        P.wait("tensor", cIn, xt)
        xr = xtm2[:, par, :].rearrange("p (c t u) -> p c t u", t=128, u=2)
        for half in (0, 1):
            for qq in range(8):
                chh = half * 8 + qq
                banks[qq].use(P, "tensor")
                P.em("tensor", lambda e, o=pb[qq][:, 0:128], i=xr[:, chh >> 1, :, chh & 1]: e.transpose(o, i, iden[:]))
            tex = drain_t()
            if half == 1:
                r_xtm[par].freed_by(cT, tex)
            P.wait("scalar", cT, tex)
            for qq in range(8):
                chh = half * 8 + qq
                P.em("scalar", lambda e, o=x[:, chh, 128 * it:128 * (it + 1)], i=pb[qq][:, 0:128], h=cA.h: e.copy(o, i).then_inc(h, 1))
                aex = cA.inc()
                banks[qq].freed_by(cA, aex)

    for wi, (wsrc, wdst, slab) in enumerate(((wq_d, scrAq, True), (wk_d, scrAk, True), (wv_d, scrAv, False), (wo_d, scrAo, True))):
        for fc in range(DC):
            par = (wi * DC + fc) % 2
            r_abuf[par].use(P, "gpsimd")
            ca = dma(abuf2[:, par, :], wsrc[128 * fc:128 * (fc + 1), :], cCa, eng="gpsimd")
            P.wait("sync", cCa, ca)
            if slab:
                wr = dma(wdst[:, :, fc, :].transpose([1, 0, 2]), abuf2[:, par, :].rearrange("p (o d) -> p o d", d=128), cWr)
            else:
                wr = dma(wdst[fc], abuf2[:, par, :], cWr)
            r_abuf[par].freed_by(cWr, wr)

    barrier()

    emit_ln(T3)
    barrier()
    tap("t_ln1", lnb[:].rearrange("p a b -> p (a b)"))

    emit_ffn(True, T3)
    barrier()
    tap("t_x1", x[:].rearrange("p a b -> p (a b)"))

    emit_ln(T3)
    barrier()
    tap("t_ln2", lnb[:].rearrange("p a b -> p (a b)"))

    emit_attn()
    barrier()
    tap("t_x2", x[:].rearrange("p a b -> p (a b)"))

    emit_ln(T2)
    barrier()
    tap("t_ln3", lnb[:].rearrange("p a b -> p (a b)"))

    emit_ffn(False, T2)
    barrier()

    for ot in range(8):
        par = ot % 2
        t0 = 256 + 128 * ot
        r_otm[par].use(P, "vector")
        xw = xtm2[:, par, :].rearrange("p (c t u) -> p c t u", t=128, u=2)
        for half in (0, 1):
            for qq in range(8):
                chh = half * 8 + qq
                banks[qq].use(P, "tensor")
                P.em("tensor", lambda e, o=pb[qq][:, 0:128], i=x[:, chh, t0:t0 + 128]: e.transpose(o, i, iden[:]))
            tex = drain_t()
            P.wait("vector", cT, tex)
            for qq in range(8):
                chh = half * 8 + qq
                P.em("vector", lambda e, o=xw[:, chh >> 1, :, chh & 1], i=pb[qq][:, 0:128], h=cV.h: e.tensor_copy(o, i).then_inc(h, 1))
                vex = cV.inc()
                banks[qq].freed_by(cV, vex)
        P.wait("sync", cV, vex)
        ot_t = dma(y_d[128 * ot:128 * (ot + 1), :], xtm2[:, par, :], cOut)
        r_otm[par].freed_by(cOut, ot_t)
    P.wait("sync", cOut, cOut.n)

    with nc.Block() as blk:
        for eng in ENGINES:
            ops = P.ops[eng]

            def body(e, ops=ops):
                for fn in ops:
                    fn(e)

            getattr(blk, eng)(body)

    for cm in sem_hold:
        cm.__exit__(None, None, None)
    return nc


_NC_CACHE = [None]


def _mkperm(nblk):
    o = np.arange(nblk)[:, None]
    p = np.arange(128)[None, :]
    return (256 * (o >> 1) + 2 * p + (o & 1)).reshape(-1)


def _host_inputs(inputs):
    xfull = np.asarray(inputs["L0_attn_hidden_states"])
    PD = _mkperm(16)
    PM = _mkperm(128)
    gpp = np.asarray(inputs["L0_ffn_gate_W_packed"])[PM]
    upp = np.asarray(inputs["L0_ffn_up_W_packed"])[PM]
    dpp = np.asarray(inputs["L0_ffn_down_W_packed"])[PD]
    shared = dict(
        wq=np.ascontiguousarray(np.asarray(inputs["L0_attn_Wq"])[PD]),
        wk=np.ascontiguousarray(np.asarray(inputs["L0_attn_Wk"])[PD]),
        wv=np.ascontiguousarray(np.asarray(inputs["L0_attn_Wv"])[PD]),
        wo=np.ascontiguousarray(np.asarray(inputs["L0_attn_Wo"])[:, PD]),
        gp=np.ascontiguousarray(gpp.T.reshape(8, 128, 128, 128).transpose(2, 1, 0, 3).reshape(128, 128, 1024)),
        gs=np.ascontiguousarray(np.asarray(inputs["L0_ffn_gate_scales"])[PM].reshape(128, 128, 8, 8).transpose(0, 3, 2, 1)),
        up=np.ascontiguousarray(upp.T.reshape(8, 128, 128, 128).transpose(2, 1, 0, 3).reshape(128, 128, 1024)),
        us=np.ascontiguousarray(np.asarray(inputs["L0_ffn_up_scales"])[PM].reshape(128, 128, 8, 8).transpose(0, 3, 2, 1)),
        dp=np.ascontiguousarray(dpp.T.reshape(64, 128, 2048)),
        ds=np.ascontiguousarray(np.asarray(inputs["L0_ffn_down_scales"])[PD].reshape(16, 128, 64, 8).transpose(2, 3, 0, 1)),
    )
    inv = 1.0 / (10000.0 ** (np.arange(0, 128, 2, dtype=np.float64) / 128.0))
    invf = np.concatenate([inv, inv])
    sgn = np.ones(128)
    sgn[:64] = -1.0

    maps = []
    for core in range(8):
        bb, qd = divmod(core, 4)
        own = xfull[bb, qd * 1024:(qd + 1) * 1024]
        if qd > 0:
            halo = xfull[bb, qd * 1024 - 256:qd * 1024]
            pos = np.arange(qd * 1024 - 256, qd * 1024 + 1024, dtype=np.float64)
        else:
            halo = xfull[bb, 0:256]
            pos = np.concatenate([np.arange(256), np.arange(1024)]).astype(np.float64)
        xc = np.ascontiguousarray(np.concatenate([halo, own], axis=0), dtype=np.float32)
        ang = invf[:, None] * pos[None, :]
        c64 = np.cos(ang)
        s64 = np.sin(ang) * sgn[:, None]
        chv = c64.astype(np.float32)
        clv = (c64 - chv).astype(np.float32)
        shv = s64.astype(np.float32)
        slv = (s64 - shv).astype(np.float32)
        m = np.zeros((128, 4, 2, 512), np.float32)
        pidx = np.arange(128)
        for c in range(4):
            gchunk = qd * 4 + c
            if gchunk == 0:
                m[:, c, :, 0:256] = -30000.0
            for qh in range(2):
                qrow = qh * 128 + pidx
                kk = np.arange(256)
                bad = qrow[:, None] < kk[None, :]
                m[:, c, qh, 256:512] = np.where(bad, -30000.0, 0.0)
        maps.append(dict(
            xin=xc,
            cosh=np.ascontiguousarray(chv), cosl=np.ascontiguousarray(clv),
            sinh=np.ascontiguousarray(shv), sinl=np.ascontiguousarray(slv),
            msk=np.ascontiguousarray(m.astype(ml_dtypes.bfloat16)),
            **shared,
        ))
    return maps


def kernel(**inputs):
    if _NC_CACHE[0] is None:
        _NC_CACHE[0] = build_nc()
    nc = _NC_CACHE[0]
    maps = _host_inputs(inputs)
    res = run_bass_kernel_spmd(nc, maps, core_ids=list(range(8)))
    out = np.empty((2, 4096, 2048), np.float32)
    for core in range(8):
        bb, qd = divmod(core, 4)
        out[bb, qd * 1024:(qd + 1) * 1024] = res.results[core]["y"]
    if _DEBUG:
        kernel.last_results = res
    return out


if __name__ == "__main__":
    nc = build_nc()
    print("built ok, instructions:", nc.n_instructions if hasattr(nc, "n_instructions") else "?")
